# version 01 (final)
# baseline (speedup 1.0000x reference)
import numpy as np
import ml_dtypes
from contextlib import ExitStack

import concourse.bass as bass
import concourse.mybir as mybir
from concourse.bass_utils import run_bass_kernel_spmd
from concourse.aot_env import allow_ici_all_to_all

F32 = mybir.dt.float32
BF16 = mybir.dt.bfloat16
AF = mybir.ActivationFunctionType
ALU = mybir.AluOpType

NCORES = 8
DM = 2048
DI = 4096
DS = 16
DTR = 128
GT = 32768
NBLK = 32
TB = 1024
OWN = 4
TOK = OWN * TB
DSH = 512
NCH = DSH // 128
FFH = 8192
FFA = 4096
NFA = FFA // 128
FI = 5632
NFI = FI // 128
KD = DM // 128
KI = DI // 128
NOC = DM // 128
EPS = 1e-5
DEV = False

BFD = ml_dtypes.bfloat16


class Sem:
    def __init__(self, es, nc, name):
        self.h = es.enter_context(nc.semaphore(name))
        self.n = 0

    def bump(self, k=1):
        self.n += k
        return self.n


def build_nc(dev=False, stage='full'):
    nc = bass.Bass("TRN2", target_bir_lowering=False, num_devices=NCORES)

    def ein(name, shape, dt=F32):
        return nc.dram_tensor(name, shape, dt, kind="ExternalInput").ap()

    x0_fm = ein("x0_fm", [DM, TOK])
    wxT = ein("wxT", [128, KD * DSH])
    wzT = ein("wzT", [128, KD * DSH])
    xpT = ein("xpT", [128, NCH * 160])
    dtpT = ein("dtpT", [128, DSH])
    opT = ein("opT", [128, NOC * KI * 128])
    ffpT = ein("ffpT", [128, NFA * 2 * KD * 128])
    ffoT = ein("ffoT", [128, NOC * KI * 128])
    gT = ein("gT", [128, NFI * KD * 128])
    uT = ein("uT", [128, NFI * KD * 128])
    dnT = ein("dnT", [128, NOC * NFI * 128])
    convw = ein("convw", [128, NCH * 4])
    convb = ein("convb", [128, NCH])
    dtbv = ein("dtbv", [128, NCH])
    alog = ein("alog", [128, NCH * DS])
    dskip = ein("dskip", [128, NCH])
    ln1w = ein("ln1w", [128, KD])
    ln1b = ein("ln1b", [128, KD])
    ln2w = ein("ln2w", [128, KD])
    ln2b = ein("ln2b", [128, KD])
    gsv = ein("gsv", [128, NFI])
    usv = ein("usv", [128, NFI])
    dnsv = ein("dnsv", [128, NOC])

    out_fm = nc.dram_tensor("out_fm", [DM, TOK], F32, kind="ExternalOutput").ap()
    if dev:
        v_dev = nc.dram_tensor("v_dev", [DM, TOK], BF16, kind="ExternalOutput").ap()
        xc_dev = nc.dram_tensor("xc_dev", [DSH, GT], BF16, kind="ExternalOutput").ap()
        dt_dev = nc.dram_tensor("dt_dev", [DSH, GT], BF16, kind="ExternalOutput").ap()
        y_dev = nc.dram_tensor("y_dev", [DSH, GT], BF16, kind="ExternalOutput").ap()
        xp_dev = nc.dram_tensor("xp_dev", [DM, TOK], F32, kind="ExternalOutput").ap()
        x1_dev = nc.dram_tensor("x1_dev", [DM, TOK], F32, kind="ExternalOutput").ap()

    def idr(name, shape, dt):
        return nc.dram_tensor(name, shape, dt, kind="Internal").ap()

    vin_c = [idr(f"vin{c}", [DM, TB], BF16) for c in range(OWN)]
    vout = [idr(f"vout{c}", [NCORES * DM, TB], BF16) for c in range(OWN)]
    ar_buf = idr("ar_buf", [NBLK // 2, 160, 2 * TB], F32)
    bcbf = idr("bcbf", [NBLK // 2, 32, 2 * TB], BF16)
    a2a_in = [idr(f"a2ain{c}", [NCORES * DSH, TB], BF16) for c in range(OWN)]
    a2a_out = [idr(f"a2aout{c}", [NCORES * DSH, TB], BF16) for c in range(OWN)]
    statbuf = idr("statbuf", [16, TOK], F32)
    xprime = idr("xprime", [DM, TOK], F32)
    blkbuf = idr("blkbuf", [DM, TOK], F32)
    x1buf = idr("x1buf", [DM, TOK], F32)
    fbuf = idr("fbuf", [DM, TOK], F32)
    WSEG = {}
    off = 0
    for nm, sz in [("op", NOC * KI * 128), ("ffp", NFA * 2 * KD * 128),
                   ("ffo", NOC * KI * 128), ("g", NFI * KD * 128),
                   ("u", NFI * KD * 128), ("dn", NOC * NFI * 128)]:
        WSEG[nm] = off
        off += sz
    wbf = idr("wbf", [128, off], BF16)

    es = ExitStack()
    with es:
        s_res = Sem(es, nc, "s_res")
        s_gw = Sem(es, nc, "s_gw")
        s_cc = Sem(es, nc, "s_cc")
        s_pe = Sem(es, nc, "s_pe")
        s_ev = Sem(es, nc, "s_ev")
        s_dv = Sem(es, nc, "s_dv")
        s_ac = Sem(es, nc, "s_ac")
        s_gp = Sem(es, nc, "s_gp")
        s_ph = Sem(es, nc, "s_ph")
        s_out = Sem(es, nc, "s_out")
        s_wc = Sem(es, nc, "s_wc")
        s_x0h = Sem(es, nc, "s_x0h")
        s_stst = Sem(es, nc, "s_stst")
        s_p0bc = Sem(es, nc, "s_p0bc")
        s_vst = Sem(es, nc, "s_vst")
        s_vl0 = Sem(es, nc, "s_vl0")
        s_vl1 = Sem(es, nc, "s_vl1")
        s_arst = Sem(es, nc, "s_arst")
        s_bcc0 = Sem(es, nc, "s_bcc0")
        s_bcc1 = Sem(es, nc, "s_bcc1")
        s_dtl = Sem(es, nc, "s_dtl")
        s_bb0 = Sem(es, nc, "s_bb0")
        s_bb1 = Sem(es, nc, "s_bb1")
        s_yst = Sem(es, nc, "s_yst")
        s_dev = Sem(es, nc, "s_dev")
        s_w0 = Sem(es, nc, "s_w0")
        s_w1 = Sem(es, nc, "s_w1")
        s_yb = Sem(es, nc, "s_yb")
        s_x0d0 = Sem(es, nc, "s_x0d0")
        s_x0d1 = Sem(es, nc, "s_x0d1")
        s_xpl0 = Sem(es, nc, "s_xpl0")
        s_xpl1 = Sem(es, nc, "s_xpl1")
        s_stl = Sem(es, nc, "s_stl")
        s_asl = Sem(es, nc, "s_asl")
        s_ptl = Sem(es, nc, "s_ptl")
        s_bcd = Sem(es, nc, "s_bcd")
        s_xst = Sem(es, nc, "s_xst")
        s_bst = Sem(es, nc, "s_bst")
        s_x1s = Sem(es, nc, "s_x1s")
        s_fst = Sem(es, nc, "s_fst")

        blk_cm = es.enter_context(nc.Block())
        ENGS = ["sync", "tensor", "vector", "scalar", "gpsimd"]

        def on(eng_name):
            return getattr(blk_cm, eng_name)

        def phase_barrier():
            tgt = s_ph.n + len(ENGS)
            for e in ENGS:
                def emit(eng, _t=tgt):
                    eng.nop().then_inc(s_ph.h, 1)
                    eng.wait_ge(s_ph.h, _t)
                on(e)(emit)
            s_ph.n = tgt

        wx_sb = es.enter_context(nc.sbuf_tensor("wx_sb", [128, KD * DSH], BF16))
        wz_sb = es.enter_context(nc.sbuf_tensor("wz_sb", [128, KD * DSH], BF16))
        xp_sb = es.enter_context(nc.sbuf_tensor("xp_sb", [128, NCH * 160], BF16))
        dtp_sb = es.enter_context(nc.sbuf_tensor("dtp_sb", [128, DSH], BF16))
        convw_sb = es.enter_context(nc.sbuf_tensor("convw_sb", [128, NCH * 4], F32))
        convb_sb = es.enter_context(nc.sbuf_tensor("convb_sb", [128, NCH], F32))
        dtb_sb = es.enter_context(nc.sbuf_tensor("dtb_sb", [128, NCH], F32))
        A_sb = es.enter_context(nc.sbuf_tensor("A_sb", [128, NCH * DS], F32))
        D_sb = es.enter_context(nc.sbuf_tensor("D_sb", [128, NCH], F32))
        ln1w_sb = es.enter_context(nc.sbuf_tensor("ln1w_sb", [128, KD], F32))
        ln1b_sb = es.enter_context(nc.sbuf_tensor("ln1b_sb", [128, KD], F32))
        ln2w_sb = es.enter_context(nc.sbuf_tensor("ln2w_sb", [128, KD], F32))
        ln2b_sb = es.enter_context(nc.sbuf_tensor("ln2b_sb", [128, KD], F32))
        nw1_sb = es.enter_context(nc.sbuf_tensor("nw1_sb", [128, KD], F32))
        nw2_sb = es.enter_context(nc.sbuf_tensor("nw2_sb", [128, KD], F32))
        gs_sb = es.enter_context(nc.sbuf_tensor("gs_sb", [128, NFI], F32))
        us_sb = es.enter_context(nc.sbuf_tensor("us_sb", [128, NFI], F32))
        dns_sb = es.enter_context(nc.sbuf_tensor("dns_sb", [128, NOC], F32))
        ones_sb = es.enter_context(nc.sbuf_tensor("ones_sb", [128, 1], BF16))
        cst_sb = es.enter_context(nc.sbuf_tensor("cst_sb", [128, 2], F32))

        def p_sync(s):
            for dst, src in [(convw_sb, convw), (convb_sb, convb), (dtb_sb, dtbv),
                             (A_sb, alog), (D_sb, dskip), (ln1w_sb, ln1w),
                             (ln1b_sb, ln1b), (ln2w_sb, ln2w), (ln2b_sb, ln2b),
                             (gs_sb, gsv), (us_sb, usv), (dns_sb, dnsv)]:
                s.dma_start(out=dst[:], in_=src).then_inc(s_res.h, 16)
        on("sync")(p_sync)
        s_res.bump(12 * 16)

        def p_gp(g):
            g.dma_start(out=wx_sb[:], in_=wxT).then_inc(s_gw.h, 16)
            g.dma_start(out=wz_sb[:], in_=wzT).then_inc(s_gw.h, 16)
            g.dma_start(out=xp_sb[:], in_=xpT).then_inc(s_gw.h, 16)
            g.dma_start(out=dtp_sb[:], in_=dtpT).then_inc(s_gw.h, 16)
        on("gpsimd")(p_gp)
        s_gw.bump(4 * 16)
        res_tag = s_res.n
        gw_tag = s_gw.n

        def p_act(sc):
            sc.wait_ge(s_res.h, res_tag)
            sc.wait_ge(s_gw.h, gw_tag)
            sc.activation(out=A_sb[:], in_=A_sb[:], func=AF.Exp)
            sc.activation(out=nw1_sb[:], in_=ln1w_sb[:], func=AF.Copy,
                          scale=-1.0)
            sc.activation(out=nw2_sb[:], in_=ln2w_sb[:], func=AF.Copy,
                          scale=-1.0).then_inc(s_ac.h, 1)
        on("scalar")(p_act)
        ac_pro = s_ac.bump()

        def p_vec(v):
            v.wait_ge(s_ac.h, ac_pro)
            v.tensor_scalar(out=A_sb[:], in0=A_sb[:], scalar1=-1.0, scalar2=None,
                            op0=ALU.mult)
            v.memset(ones_sb[:], 1.0)
            v.memset(cst_sb[:, 0:1], EPS)
            v.memset(cst_sb[:, 1:2], 1.0).then_inc(s_dv.h, 1)
        on("vector")(p_vec)
        dv_pro = s_dv.bump()

        def p_wcast(g):
            for nm, src in [("op", opT), ("ffp", ffpT), ("ffo", ffoT),
                            ("g", gT), ("u", uT), ("dn", dnT)]:
                o = WSEG[nm]
                g.dma_start(out=wbf[:, o:o + src.shape[1]],
                            in_=src).then_inc(s_wc.h, 16)
        on("gpsimd")(p_wcast)
        s_wc.bump(6 * 16)
        wc_tag = s_wc.n

        def emit_stats_round(src_tile_fn, nk, cb_par, sq_par, ps_sum, ps_sq,
                             wait_first_fn=None):
            pe_base = s_pe.n
            for k in range(nk):
                par = k % 2

                def e_cast(sc, k=k, par=par, pe_base=pe_base):
                    if wait_first_fn is not None and k == 0:
                        wait_first_fn(sc)
                    if k >= 2:
                        sc.wait_ge(s_pe.h, pe_base + k - 1)
                    src = src_tile_fn(k)
                    sc.activation(out=cb_par[:, par * 512:(par + 1) * 512],
                                  in_=src, func=AF.Copy)
                    sc.activation(out=sq_par[:, par * 512:(par + 1) * 512],
                                  in_=src, func=AF.Square).then_inc(s_ac.h, 1)
                on("scalar")(e_cast)
                ac_tag = s_ac.bump()

                def e_mm(t, k=k, par=par, ac_tag=ac_tag):
                    t.wait_ge(s_ac.h, ac_tag)
                    t.matmul(ps_sum[:1, :], ones_sb[:],
                             cb_par[:, par * 512:(par + 1) * 512],
                             start=(k == 0), stop=(k == nk - 1))
                    t.matmul(ps_sq[:1, :], ones_sb[:],
                             sq_par[:, par * 512:(par + 1) * 512],
                             start=(k == 0),
                             stop=(k == nk - 1)).then_inc(s_pe.h, 1)
                on("tensor")(e_mm)
                s_pe.bump()

        def emit_stats_finalize(strow_t, ps_sum, ps_sq, mrow, rrow, colbase,
                                extra_mr_row=None):
            pe_tag = s_pe.n

            def e_fin(v, pe_tag=pe_tag):
                v.wait_ge(s_pe.h, pe_tag)
                v.tensor_scalar(out=strow_t[:1, 0:512], in0=ps_sum[:1, :],
                                scalar1=1.0 / DM, scalar2=None, op0=ALU.mult)
                v.tensor_scalar(out=strow_t[:1, 512:1024], in0=ps_sq[:1, :],
                                scalar1=1.0 / DM, scalar2=None, op0=ALU.mult)
                v.tensor_tensor(out=strow_t[:1, 1024:1536],
                                in0=strow_t[:1, 0:512],
                                in1=strow_t[:1, 0:512], op=ALU.mult)
                v.tensor_sub(strow_t[:1, 512:1024], strow_t[:1, 512:1024],
                             strow_t[:1, 1024:1536]).then_inc(s_dv.h, 1)
            on("vector")(e_fin)
            dv_tag = s_dv.bump()

            def e_rstd(sc, dv_tag=dv_tag):
                sc.wait_ge(s_dv.h, dv_tag)
                sc.activation(out=strow_t[:1, 512:1024],
                              in_=strow_t[:1, 512:1024], func=AF.Ln,
                              bias=cst_sb[:1, 0:1])
                sc.activation(out=strow_t[:1, 512:1024],
                              in_=strow_t[:1, 512:1024], func=AF.Exp,
                              scale=-0.5).then_inc(s_ac.h, 1)
            on("scalar")(e_rstd)
            ac_tag = s_ac.bump()

            if extra_mr_row is not None:
                def e_mr(v, ac_tag=ac_tag):
                    v.wait_ge(s_ac.h, ac_tag)
                    v.tensor_tensor(out=strow_t[:1, 1024:1536],
                                    in0=strow_t[:1, 0:512],
                                    in1=strow_t[:1, 512:1024],
                                    op=ALU.mult).then_inc(s_dv.h, 1)
                on("vector")(e_mr)
                mr_tag = s_dv.bump()
            else:
                mr_tag = None

            def e_store(s, ac_tag=ac_tag, mr_tag=mr_tag):
                s.wait_ge(s_ac.h, ac_tag)
                if mr_tag is not None:
                    s.wait_ge(s_dv.h, mr_tag)
                s.dma_start(out=statbuf[mrow:mrow + 1, colbase:colbase + 512],
                            in_=strow_t[:1, 0:512]).then_inc(s_stst.h, 16)
                s.dma_start(out=statbuf[rrow:rrow + 1, colbase:colbase + 512],
                            in_=strow_t[:1, 512:1024]).then_inc(s_stst.h, 16)
                if extra_mr_row is not None:
                    s.dma_start(
                        out=statbuf[extra_mr_row:extra_mr_row + 1,
                                    colbase:colbase + 512],
                        in_=strow_t[:1, 1024:1536]).then_inc(s_stst.h, 16)
            on("sync")(e_store)
            s_stst.bump(16 * (3 if extra_mr_row is not None else 2))
            return s_stst.n

        agv_tag = []
        with (
            nc.sbuf_tensor("x0h", [128, KD * 512], F32) as x0h,
            nc.sbuf_tensor("u0h", [128, KD * 512], F32) as u0h,
            nc.sbuf_tensor("vh", [128, KD * 512], BF16) as vh,
            nc.sbuf_tensor("cb2", [128, 2 * 512], BF16) as cb2,
            nc.sbuf_tensor("sq2", [128, 2 * 512], BF16) as sq2,
            nc.sbuf_tensor("mB0", [128, 512], F32) as mB0,
            nc.sbuf_tensor("rB0", [128, 512], F32) as rB0,
            nc.sbuf_tensor("strow", [1, 2048], F32) as strow,
            nc.psum_tensor("ps_su", [1, 512], F32) as ps_su,
            nc.psum_tensor("ps_sq", [1, 512], F32) as ps_sq,
        ):
            prev_half_done = None
            for c in range(OWN):
                for hh in range(2):
                    colb = c * TB + hh * 512

                    def e_ld(s, colb=colb, pv=prev_half_done):
                        s.wait_ge(s_res.h, res_tag)
                        if pv is not None:
                            s.wait_ge(s_vst.h, pv)
                        s.dma_start(
                            out=x0h[:].rearrange("p (k t) -> p k t", t=512),
                            in_=x0_fm[:, colb:colb + 512].rearrange(
                                "(k p) t -> p k t", p=128)).then_inc(s_x0h.h, 16)
                    on("sync")(e_ld)
                    ld_tag = s_x0h.bump(16)

                    def wf(sc, ld_tag=ld_tag):
                        sc.wait_ge(s_x0h.h, ld_tag)
                        sc.wait_ge(s_dv.h, dv_pro)

                    emit_stats_round(lambda k: x0h[:, k * 512:(k + 1) * 512],
                                     KD, cb2, sq2, ps_su, ps_sq, wait_first_fn=wf)
                    st_tag = emit_stats_finalize(strow, ps_su, ps_sq, 0, 1, colb)

                    def e_bc(sc, colb=colb, st_tag=st_tag):
                        sc.wait_ge(s_stst.h, st_tag)
                        sc.dma_start(out=mB0[:], in_=statbuf[0:1, colb:colb + 512]
                                     .to_broadcast([128, 512])
                                     ).then_inc(s_p0bc.h, 16)
                        sc.dma_start(out=rB0[:], in_=statbuf[1:2, colb:colb + 512]
                                     .to_broadcast([128, 512])
                                     ).then_inc(s_p0bc.h, 16)
                    on("scalar")(e_bc)
                    bc_tag = s_p0bc.bump(32)

                    def e_u0(v, bc_tag=bc_tag):
                        v.wait_ge(s_p0bc.h, bc_tag)
                        for k in range(KD):
                            sl = slice(k * 512, (k + 1) * 512)
                            v.tensor_sub(u0h[:, sl], x0h[:, sl], mB0[:])
                            i = v.tensor_mul(u0h[:, sl], u0h[:, sl], rB0[:])
                        i.then_inc(s_dv.h, 1)
                    on("vector")(e_u0)
                    u0_tag = s_dv.bump()

                    def wf2(sc, u0_tag=u0_tag):
                        sc.wait_ge(s_dv.h, u0_tag)

                    emit_stats_round(lambda k: u0h[:, k * 512:(k + 1) * 512],
                                     KD, cb2, sq2, ps_su, ps_sq, wait_first_fn=wf2)
                    st_tag2 = emit_stats_finalize(strow, ps_su, ps_sq, 12, 13,
                                                  colb, extra_mr_row=9)

                    def e_bc2(sc, colb=colb, st_tag2=st_tag2):
                        sc.wait_ge(s_stst.h, st_tag2)
                        sc.dma_start(out=mB0[:], in_=statbuf[9:10, colb:colb + 512]
                                     .to_broadcast([128, 512])
                                     ).then_inc(s_p0bc.h, 16)
                        sc.dma_start(out=rB0[:], in_=statbuf[13:14, colb:colb + 512]
                                     .to_broadcast([128, 512])
                                     ).then_inc(s_p0bc.h, 16)
                    on("scalar")(e_bc2)
                    bc2_tag = s_p0bc.bump(32)

                    def e_v(v, bc2_tag=bc2_tag):
                        v.wait_ge(s_p0bc.h, bc2_tag)
                        for k in range(KD):
                            sl = slice(k * 512, (k + 1) * 512)
                            v.scalar_tensor_tensor(
                                out=u0h[:, sl], in0=u0h[:, sl],
                                scalar=ln1w_sb[:, k:k + 1], in1=rB0[:],
                                op0=ALU.mult, op1=ALU.mult)
                            i = v.scalar_tensor_tensor(
                                out=u0h[:, sl], in0=mB0[:],
                                scalar=nw1_sb[:, k:k + 1], in1=u0h[:, sl],
                                op0=ALU.mult, op1=ALU.add)
                        i.then_inc(s_dv.h, 1)
                    on("vector")(e_v)
                    v_tag = s_dv.bump()

                    def e_vb(sc, v_tag=v_tag):
                        sc.wait_ge(s_dv.h, v_tag)
                        for k in range(KD):
                            sl = slice(k * 512, (k + 1) * 512)
                            i = sc.activation(out=vh[:, sl], in_=u0h[:, sl],
                                              func=AF.Identity,
                                              bias=ln1b_sb[:, k:k + 1])
                        i.then_inc(s_ac.h, 1)
                    on("scalar")(e_vb)
                    vb_tag = s_ac.bump()

                    def e_vst(s, c=c, hh=hh, vb_tag=vb_tag):
                        s.wait_ge(s_ac.h, vb_tag)
                        col = hh * 512
                        s.dma_start(
                            out=vin_c[c][:, col:col + 512].rearrange(
                                "(k p) t -> p k t", p=128),
                            in_=vh[:].rearrange("p (k t) -> p k t", t=512)
                            ).then_inc(s_vst.h, 16)
                    on("sync")(e_vst)
                    s_vst.bump(16)
                    if dev:
                        def e_vdev(s, c=c, hh=hh):
                            col = c * TB + hh * 512
                            s.dma_start(
                                out=v_dev[:, col:col + 512].rearrange(
                                    "(k p) t -> p k t", p=128),
                                in_=vh[:].rearrange("p (k t) -> p k t", t=512)
                                ).then_inc(s_vst.h, 16)
                        on("sync")(e_vdev)
                        s_vst.bump(16)
                    prev_half_done = s_vst.n

                vdone = s_vst.n

                def e_ag(g, c=c, vdone=vdone):
                    g.wait_ge(s_vst.h, vdone)
                    g.collective_compute(
                        "AllGather", ALU.bypass,
                        replica_groups=[list(range(NCORES))],
                        ins=[vin_c[c]],
                        outs=[vout[c]]).then_inc(s_cc.h, 1)
                on("gpsimd")(e_ag)
                agv_tag.append(s_cc.bump())

        if stage == "p0":
            agv_final = s_cc.n

            def finp0(g):
                g.wait_ge(s_cc.h, agv_final)
            on("gpsimd")(finp0)
            return nc

        phase_barrier()

        GRP = 2
        NGRP = NBLK // GRP
        a2a_tag = []
        scan_es = ExitStack()
        vhalf = scan_es.enter_context(nc.sbuf_tensor("vhalf", [128, 2 * KD * 512], BF16))
        xm_t = scan_es.enter_context(nc.sbuf_tensor("xm_t", [128, NCH * 1027], F32))
        cacc = scan_es.enter_context(nc.sbuf_tensor("cacc", [128, NCH * 1024], F32))
        xc_bf = scan_es.enter_context(nc.sbuf_tensor("xc_bf", [128, GRP * NCH * 1024], BF16))
        sz_bf = scan_es.enter_context(nc.sbuf_tensor("sz_bf", [128, GRP * NCH * 1024], BF16))
        dt_bf = scan_es.enter_context(nc.sbuf_tensor("dt_bf", [128, GRP * NCH * 1024], BF16))
        dtx_bf = scan_es.enter_context(nc.sbuf_tensor("dtx_bf", [128, NCH * 1024], BF16))
        dtin_bf = scan_es.enter_context(nc.sbuf_tensor("dtin_bf", [128, 1024], BF16))
        spt = scan_es.enter_context(nc.sbuf_tensor("spt", [128, 2 * 512], F32))
        spt2 = scan_es.enter_context(nc.sbuf_tensor("spt2", [128, 2 * 512], F32))
        spt3 = scan_es.enter_context(nc.sbuf_tensor("spt3", [128, 512], F32))
        dA_t = scan_es.enter_context(nc.sbuf_tensor("dA_t", [128, 2 * 1026], F32))
        Bb_t = scan_es.enter_context(nc.sbuf_tensor("Bb_t", [128, 2 * 1024], BF16))
        Cb_t = scan_es.enter_context(nc.sbuf_tensor("Cb_t", [128, 2 * 1024], BF16))
        b_bf = scan_es.enter_context(nc.sbuf_tensor("b_bf", [128, 1026], BF16))
        h_bf = scan_es.enter_context(nc.sbuf_tensor("h_bf", [128, 2 * 1026], BF16))
        hctmp = scan_es.enter_context(nc.sbuf_tensor("hctmp", [128, 2 * 1024], BF16))
        ys32 = scan_es.enter_context(nc.sbuf_tensor("ys32", [128, 1024], F32))
        y_bf = scan_es.enter_context(nc.sbuf_tensor("y_bf", [128, NCH * 1024], BF16))
        hstate = scan_es.enter_context(nc.sbuf_tensor("hstate", [128, NCH * DS], F32))
        ps_a = scan_es.enter_context(nc.psum_tensor("ps_a", [128, 512], F32))
        ps_b = scan_es.enter_context(nc.psum_tensor("ps_b", [128, 512], F32))
        ps_c = scan_es.enter_context(nc.psum_tensor("ps_c", [128, 512], F32))
        ps_d = scan_es.enter_context(nc.psum_tensor("ps_d", [128, 512], F32))
        ps_x = scan_es.enter_context(nc.psum_tensor("ps_x", [128, 512], F32))
        ps_y = scan_es.enter_context(nc.psum_tensor("ps_y", [32, 512], F32))
        if True:
            ps4 = [ps_a, ps_b, ps_c, ps_d]

            def e_init(v):
                v.memset(hstate[:], 0.0)
                v.memset(dA_t[:, 0:2], 0.0)
                v.memset(dA_t[:, 1026:1028], 0.0)
                v.memset(b_bf[:, 0:1], 0.0)
                v.memset(xm_t[:], 0.0).then_inc(s_dv.h, 1)
            on("vector")(e_init)
            s_dv.bump()

            def e_winit(t):
                t.wait_ge(s_gw.h, gw_tag)
            on("tensor")(e_winit)

            vhalf_free = [None, None]
            last_zev = None
            last_conv = None
            last_xc = {}
            last_gate = {}
            last_yst = None
            last_dt_read = {}
            last_ps4_read = None
            last_xdev = None
            last_psxy = None
            last_dtp = None
            last_sp2 = [None]

            for g in range(NGRP):
                blocks = [g * GRP + i for i in range(GRP)]
                for bi, b in enumerate(blocks):
                    agc = b // 8
                    brow = (b % 8) * DM
                    if b in (0, 16):
                        def e_rst(v, b=b):
                            for cch in range(NCH):
                                i = v.memset(xm_t[:, cch * 1027:
                                                  cch * 1027 + 3], 0.0)
                            i.then_inc(s_dv.h, 1)
                        on("vector")(e_rst)
                        s_dv.bump()

                    for half in range(2):
                        vslot = half
                        vslot = (b * 2 + half) % 2
                        free_tag = vhalf_free[vslot]

                        svl = s_vl0 if vslot == 0 else s_vl1

                        def e_vld(s, agc=agc, brow=brow, half=half, vslot=vslot,
                                  free_tag=free_tag):
                            s.wait_ge(s_cc.h, agv_tag[agc])
                            if free_tag is not None:
                                s.wait_ge(s_pe.h, free_tag)
                            s.dma_start(
                                out=vhalf[:, vslot * KD * 512:
                                          (vslot + 1) * KD * 512]
                                .rearrange("p (k t) -> p k t", t=512),
                                in_=vout[agc][brow:brow + DM,
                                              half * 512:half * 512 + 512]
                                .rearrange("(k p) t -> p k t", p=128)
                            ).then_inc(svl.h, 16)
                        on("sync")(e_vld)
                        vld_tag = svl.bump(16)

                        def e_xm(t, vslot=vslot, vld_tag=vld_tag, svl=svl,
                                 lp=last_ps4_read, lc=last_conv):
                            t.wait_ge(svl.h, vld_tag)
                            if lp is not None:
                                kind, tg = lp
                                t.wait_ge(s_ev.h if kind == "ev" else s_ac.h, tg)
                            if lc is not None:
                                t.wait_ge(s_dv.h, lc)
                            for k in range(KD):
                                for cch in range(NCH):
                                    i = t.matmul(
                                        ps4[cch][:],
                                        wx_sb[:, k * DSH + cch * 128:
                                              k * DSH + (cch + 1) * 128],
                                        vhalf[:, vslot * KD * 512 + k * 512:
                                              vslot * KD * 512 + (k + 1) * 512],
                                        start=(k == 0), stop=(k == KD - 1))
                            i.then_inc(s_pe.h, 1)
                        on("tensor")(e_xm)
                        xm_tag = s_pe.bump()

                        def e_xmev(sc, xm_tag=xm_tag, tcol=half * 512):
                            sc.wait_ge(s_pe.h, xm_tag)
                            for cch in range(NCH):
                                i = sc.activation(
                                    out=xm_t[:, cch * 1027 + 3 + tcol:
                                             cch * 1027 + 3 + tcol + 512],
                                    in_=ps4[cch][:], func=AF.Copy)
                            i.then_inc(s_ev.h, 1)
                        on("scalar")(e_xmev)
                        xmev_tag = s_ev.bump()

                        def e_z(t, vslot=vslot, xmev_tag=xmev_tag):
                            t.wait_ge(s_ev.h, xmev_tag)
                            for k in range(KD):
                                for cch in range(NCH):
                                    i = t.matmul(
                                        ps4[cch][:],
                                        wz_sb[:, k * DSH + cch * 128:
                                              k * DSH + (cch + 1) * 128],
                                        vhalf[:, vslot * KD * 512 + k * 512:
                                              vslot * KD * 512 + (k + 1) * 512],
                                        start=(k == 0), stop=(k == KD - 1))
                            i.then_inc(s_pe.h, 1)
                        on("tensor")(e_z)
                        z_tag = s_pe.bump()
                        vhalf_free[vslot] = z_tag

                        lg = last_gate.get(bi)

                        def e_zev(sc, z_tag=z_tag, bi=bi, tcol=half * 512, lg=lg):
                            sc.wait_ge(s_pe.h, z_tag)
                            if lg is not None:
                                sc.wait_ge(s_dv.h, lg)
                            for cch in range(NCH):
                                i = sc.activation(
                                    out=sz_bf[:, (bi * NCH + cch) * 1024 + tcol:
                                              (bi * NCH + cch) * 1024 + tcol + 512],
                                    in_=ps4[cch][:], func=AF.Silu)
                            i.then_inc(s_ev.h, 1)
                        on("scalar")(e_zev)
                        last_zev = s_ev.bump()
                        last_ps4_read = ("ev", last_zev)

                    lx = last_xc.get("cacc")

                    def e_conv(v, lz=last_zev, lx=lx):
                        v.wait_ge(s_ev.h, lz)
                        if lx is not None:
                            v.wait_ge(s_ac.h, lx)
                        for cch in range(NCH):
                            v.tensor_scalar(
                                out=cacc[:, cch * 1024:(cch + 1) * 1024],
                                in0=xm_t[:, cch * 1027:cch * 1027 + 1024],
                                scalar1=convw_sb[:, cch * 4:cch * 4 + 1],
                                scalar2=None, op0=ALU.mult)
                            for j in range(1, 4):
                                v.scalar_tensor_tensor(
                                    out=cacc[:, cch * 1024:(cch + 1) * 1024],
                                    in0=xm_t[:, cch * 1027 + j:
                                             cch * 1027 + j + 1024],
                                    scalar=convw_sb[:, cch * 4 + j:
                                                    cch * 4 + j + 1],
                                    in1=cacc[:, cch * 1024:(cch + 1) * 1024],
                                    op0=ALU.mult, op1=ALU.add)
                        for cch in range(NCH):
                            i = v.tensor_copy(
                                out=xm_t[:, cch * 1027:cch * 1027 + 3],
                                in_=xm_t[:, cch * 1027 + 1024:cch * 1027 + 1027])
                        i.then_inc(s_dv.h, 1)
                    on("vector")(e_conv)
                    last_conv = s_dv.bump()

                    lg = last_gate.get(bi)

                    def e_xc(sc, lc=last_conv, bi=bi, lg=lg):
                        sc.wait_ge(s_dv.h, lc)
                        if lg is not None:
                            sc.wait_ge(s_dv.h, lg)
                        for cch in range(NCH):
                            i = sc.activation(
                                out=xc_bf[:, (bi * NCH + cch) * 1024:
                                          (bi * NCH + cch + 1) * 1024],
                                in_=cacc[:, cch * 1024:(cch + 1) * 1024],
                                func=AF.Silu, bias=convb_sb[:, cch:cch + 1])
                        i.then_inc(s_ac.h, 1)
                    on("scalar")(e_xc)
                    xc_tag = s_ac.bump()
                    last_xc["cacc"] = xc_tag
                    last_xc[bi] = xc_tag

                    if dev:
                        def e_xcdev(s, b=b, bi=bi, xc_tag=xc_tag):
                            s.wait_ge(s_ac.h, xc_tag)
                            s.dma_start(
                                out=xc_dev[:, b * TB:(b + 1) * TB].rearrange(
                                    "(q p) t -> p q t", p=128),
                                in_=xc_bf[:, bi * NCH * 1024:
                                          (bi + 1) * NCH * 1024]
                                .rearrange("p (q t) -> p q t", t=1024)
                            ).then_inc(s_dev.h, 16)
                        on("sync")(e_xcdev)
                        s_dev.bump(16)

                    if stage == "scanA0":
                        continue
                    for half in range(2):
                        xdpar = half

                        def e_xd(t, bi=bi, tcol=half * 512, xc_tag=xc_tag,
                                 lxe=last_xdev, lpsxy=last_psxy):
                            t.wait_ge(s_ac.h, xc_tag)
                            if lxe is not None:
                                t.wait_ge(s_arst.h, lxe)
                            if lpsxy is not None:
                                t.wait_ge(s_ev.h, lpsxy)
                            for cch in range(NCH):
                                rhs = xc_bf[:, (bi * NCH + cch) * 1024 + tcol:
                                            (bi * NCH + cch) * 1024 + tcol + 512]
                                t.matmul(ps_x[:],
                                         xp_sb[:, cch * 160:cch * 160 + 128],
                                         rhs, start=(cch == 0),
                                         stop=(cch == NCH - 1))
                                i = t.matmul(ps_y[:],
                                             xp_sb[:, cch * 160 + 128:
                                                   (cch + 1) * 160],
                                             rhs, start=(cch == 0),
                                             stop=(cch == NCH - 1))
                            i.then_inc(s_pe.h, 1)
                        on("tensor")(e_xd)
                        xd_tag = s_pe.bump()

                        def e_xdev(sc, xd_tag=xd_tag, xdpar=xdpar,
                                   lsp2=last_sp2[0]):
                            sc.wait_ge(s_pe.h, xd_tag)
                            if lsp2 is not None:
                                sc.wait_ge(s_dv.h, lsp2)
                            sc.activation(out=spt[:, xdpar * 512:
                                                  (xdpar + 1) * 512],
                                          in_=ps_x[:], func=AF.Copy)
                            sc.activation(out=spt2[:32, xdpar * 512:
                                                   (xdpar + 1) * 512],
                                          in_=ps_y[:],
                                          func=AF.Copy).then_inc(s_ev.h, 1)
                        on("scalar")(e_xdev)
                        xdev_tag = s_ev.bump()
                        last_psxy = xdev_tag

                        def e_xdst(s, b=b, tcol=half * 512, xdev_tag=xdev_tag,
                                   xdpar=xdpar):
                            s.wait_ge(s_ev.h, xdev_tag)
                            g3 = b // GRP
                            col = (b % GRP) * TB + tcol
                            s.dma_start(out=ar_buf[g3, 0:128, col:col + 512],
                                        in_=spt[:, xdpar * 512:(xdpar + 1) * 512]
                                        ).then_inc(s_arst.h, 16)
                            s.dma_start(out=ar_buf[g3, 128:160, col:col + 512],
                                        in_=spt2[:32, xdpar * 512:(xdpar + 1) * 512]
                                        ).then_inc(s_arst.h, 16)
                        on("sync")(e_xdst)
                        s_arst.bump(32)
                        if half == 1:
                            last_xdev = s_arst.n

                if stage == "scanA0":
                    continue
                ar_in_tag = s_arst.n

                def e_ar(gp, g=g, ar_in_tag=ar_in_tag):
                    gp.wait_ge(s_arst.h, ar_in_tag)
                    gp.collective_compute(
                        "AllReduce", ALU.add,
                        replica_groups=[list(range(NCORES))],
                        ins=[ar_buf[g]], outs=[ar_buf[g]]
                    ).then_inc(s_cc.h, 1)
                on("gpsimd")(e_ar)
                ar_tag = s_cc.bump()

                if stage == "scanA1":
                    continue

                s_bcc = s_bcc0 if g % 2 == 0 else s_bcc1

                def e_bccast(gp, g=g, ar_tag=ar_tag, s_bcc=s_bcc):
                    gp.wait_ge(s_cc.h, ar_tag)
                    gp.dma_start(
                        out=bcbf[g], in_=ar_buf[g, 128:160, :]
                    ).then_inc(s_bcc.h, 16)
                on("gpsimd")(e_bccast)
                bccast_tag = s_bcc.bump(16)

                if stage == "scanA":
                    continue
                for bi, b in enumerate(blocks):
                    def e_dtld(gp, b=b, ar_tag=ar_tag, ldtp=last_dtp):
                        gp.wait_ge(s_cc.h, ar_tag)
                        if ldtp is not None:
                            gp.wait_ge(s_pe.h, ldtp)
                        gp.dma_start(out=dtin_bf[:],
                                     in_=ar_buf[b // GRP, 0:128,
                                                (b % GRP) * TB:
                                                (b % GRP) * TB + TB]
                                     ).then_inc(s_dtl.h, 16)
                    on("gpsimd")(e_dtld)
                    dtld_tag = s_dtl.bump(16)

                    ldt = last_dt_read.get(bi)
                    sp_it = 0
                    sp2_tags = []
                    for half in range(2):
                        def e_dtp(t, tcol=half * 512, dtld_tag=dtld_tag,
                                  lp=last_ps4_read):
                            t.wait_ge(s_dtl.h, dtld_tag)
                            if lp is not None:
                                kind, tg = lp
                                t.wait_ge(s_ev.h if kind == "ev" else s_ac.h, tg)
                            for cch in range(NCH):
                                i = t.matmul(ps4[cch][:],
                                             dtp_sb[:, cch * 128:(cch + 1) * 128],
                                             dtin_bf[:, tcol:tcol + 512],
                                             start=True, stop=True)
                            i.then_inc(s_pe.h, 1)
                        on("tensor")(e_dtp)
                        dtp_tag = s_pe.bump()
                        last_dtp = dtp_tag

                        for cch in range(NCH):
                            sppar = sp_it % 2
                            wsp = sp2_tags[sp_it - 2] if sp_it >= 2 else None

                            def e_sp1(sc, cch=cch, dtp_tag=dtp_tag, wsp=wsp,
                                      sppar=sppar):
                                sc.wait_ge(s_pe.h, dtp_tag)
                                if wsp is not None:
                                    sc.wait_ge(s_dv.h, wsp)
                                sl = slice(sppar * 512, (sppar + 1) * 512)
                                sc.activation(out=spt[:, sl], in_=ps4[cch][:],
                                              func=AF.Identity,
                                              bias=dtb_sb[:, cch:cch + 1])
                                sc.activation(out=spt2[:, sl], in_=spt[:, sl],
                                              func=AF.Abs)
                                sc.activation(out=spt2[:, sl], in_=spt2[:, sl],
                                              func=AF.Exp, scale=-1.0)
                                sc.activation(out=spt2[:, sl], in_=spt2[:, sl],
                                              func=AF.Ln,
                                              bias=cst_sb[:, 1:2]
                                              ).then_inc(s_ac.h, 1)
                            on("scalar")(e_sp1)
                            sp1_tag = s_ac.bump()
                            last_ps4_read = ("ac", sp1_tag)

                            def e_sp2(v, cch=cch, bi=bi, tcol=half * 512,
                                      sp1_tag=sp1_tag, ldt=ldt, sppar=sppar,
                                      first=(sp_it == 0)):
                                v.wait_ge(s_ac.h, sp1_tag)
                                if first and ldt is not None:
                                    v.wait_ge(s_ac.h, ldt)
                                sl = slice(sppar * 512, (sppar + 1) * 512)
                                v.tensor_scalar(out=spt3[:], in0=spt[:, sl],
                                                scalar1=0.0, scalar2=None,
                                                op0=ALU.max)
                                v.tensor_tensor(
                                    out=dt_bf[:, (bi * NCH + cch) * 1024 + tcol:
                                              (bi * NCH + cch) * 1024 + tcol + 512],
                                    in0=spt3[:], in1=spt2[:, sl],
                                    op=ALU.add).then_inc(s_dv.h, 1)
                            on("vector")(e_sp2)
                            sp2_tags.append(s_dv.bump())
                            last_sp2[0] = sp2_tags[-1]
                            sp_it += 1
                    dt_tag = s_dv.n

                    if dev:
                        def e_dtdev(s, b=b, bi=bi, dt_tag=dt_tag):
                            s.wait_ge(s_dv.h, dt_tag)
                            s.dma_start(
                                out=dt_dev[:, b * TB:(b + 1) * TB].rearrange(
                                    "(q p) t -> p q t", p=128),
                                in_=dt_bf[:, bi * NCH * 1024:
                                          (bi + 1) * NCH * 1024]
                                .rearrange("p (q t) -> p q t", t=1024)
                            ).then_inc(s_dev.h, 16)
                        on("sync")(e_dtdev)
                        s_dev.bump(16)

                    if stage == "scanB":
                        continue

                    def e_dtx(v, bi=bi, xt=last_xc[bi]):
                        v.wait_ge(s_ac.h, xt)
                        v.tensor_tensor(
                            out=dtx_bf[:],
                            in0=dt_bf[:, bi * NCH * 1024:(bi + 1) * NCH * 1024],
                            in1=xc_bf[:, bi * NCH * 1024:(bi + 1) * NCH * 1024],
                            op=ALU.mult).then_inc(s_dv.h, 1)
                    on("vector")(e_dtx)
                    s_dv.bump()

                    scan_tags = []
                    pool_tags = []
                    prev_gate_tag = None
                    pend_carry = None

                    def emit_carry(pc):
                        pc_par2, pc_col, pc_tag = pc

                        def e_carry(sc, par2=pc_par2, col=pc_col, smid=pc_tag):
                            sc.wait_ge(s_dv.h, smid)
                            sc.activation(
                                out=hstate[:, col:col + 1],
                                in_=h_bf[:, par2 * 1026 + 1025:
                                         par2 * 1026 + 1026],
                                func=AF.Copy, scale=1.0).then_inc(s_ac.h, 1)
                        on("scalar")(e_carry)
                        s_ac.bump()

                    for cch in range(NCH):
                        for s_i in range(DS):
                            it = cch * DS + s_i
                            par2 = it % 2
                            par4 = it % 2
                            wdA = scan_tags[it - 2] if it >= 2 else None
                            wbc = scan_tags[it - 2] if it >= 2 else None

                            def e_da(sc, bi=bi, cch=cch, s_i=s_i, par2=par2,
                                     wdA=wdA, first=(it == 0), dt_tag=dt_tag):
                                if first:
                                    sc.wait_ge(s_dv.h, dt_tag)
                                if wdA is not None:
                                    sc.wait_ge(s_dv.h, wdA)
                                sc.activation(
                                    out=dA_t[:, par2 * 1026 + 2:
                                             (par2 + 1) * 1026],
                                    in_=dt_bf[:, (bi * NCH + cch) * 1024:
                                              (bi * NCH + cch + 1) * 1024],
                                    func=AF.Exp,
                                    scale=A_sb[:, cch * DS + s_i:
                                               cch * DS + s_i + 1]
                                ).then_inc(s_ac.h, 1)
                            on("scalar")(e_da)
                            da_tag = s_ac.bump()
                            last_dt_read[bi] = da_tag

                            s_bbp = s_bb0 if par4 == 0 else s_bb1

                            def e_bb(sc, b=b, s_i=s_i, par4=par4, wbc=wbc,
                                     bccast_tag=bccast_tag, first=(it == 0),
                                     s_bcc=s_bcc, s_bbp=s_bbp):
                                if first:
                                    sc.wait_ge(s_bcc.h, bccast_tag)
                                if wbc is not None:
                                    sc.wait_ge(s_dv.h, wbc)
                                g3 = b // GRP
                                colb = (b % GRP) * TB
                                sc.dma_start(
                                    out=Bb_t[:, par4 * 1024:(par4 + 1) * 1024],
                                    in_=bcbf[g3, s_i:s_i + 1, colb:colb + TB]
                                    .to_broadcast([128, TB])
                                    ).then_inc(s_bbp.h, 16)
                                sc.dma_start(
                                    out=Cb_t[:, par4 * 1024:(par4 + 1) * 1024],
                                    in_=bcbf[g3, 16 + s_i:17 + s_i,
                                             colb:colb + TB]
                                    .to_broadcast([128, TB])
                                    ).then_inc(s_bbp.h, 16)
                            on("scalar")(e_bb)
                            bb_tag = s_bbp.bump(32)

                            if pend_carry is not None:
                                emit_carry(pend_carry)
                                pend_carry = None

                            whc = pool_tags[it - 2] if it >= 2 else None

                            def e_scan(v, cch=cch, s_i=s_i, par2=par2, par4=par4,
                                       da_tag=da_tag, bb_tag=bb_tag, whc=whc,
                                       s_bbp=s_bbp, zc=(b in (0, 16))):
                                v.wait_ge(s_ac.h, da_tag)
                                v.wait_ge(s_bbp.h, bb_tag)
                                if whc is not None:
                                    v.wait_ge(s_gp.h, whc)
                                if zc:
                                    v.memset(b_bf[:, 1:2], 0.0)
                                else:
                                    v.tensor_copy(
                                        out=b_bf[:, 1:2],
                                        in_=hstate[:, cch * DS + s_i:
                                                   cch * DS + s_i + 1])
                                v.tensor_tensor(
                                    out=b_bf[:, 2:],
                                    in0=dtx_bf[:, cch * 1024:(cch + 1) * 1024],
                                    in1=Bb_t[:, par4 * 1024:(par4 + 1) * 1024],
                                    op=ALU.mult)
                                v.tensor_tensor_scan(
                                    out=h_bf[:, par2 * 1026:(par2 + 1) * 1026],
                                    data0=dA_t[:, par2 * 1026:(par2 + 1) * 1026],
                                    data1=b_bf[:],
                                    initial=0.0,
                                    op0=ALU.mult, op1=ALU.add).then_inc(s_dv.h, 1)
                                v.tensor_tensor(
                                    out=hctmp[:, par2 * 1024:(par2 + 1) * 1024],
                                    in0=h_bf[:, par2 * 1026 + 2:
                                             (par2 + 1) * 1026],
                                    in1=Cb_t[:, par4 * 1024:(par4 + 1) * 1024],
                                    op=ALU.mult).then_inc(s_dv.h, 1)
                            on("vector")(e_scan)
                            scan_mid = s_dv.bump()
                            scan_tags.append(s_dv.bump())
                            pend_carry = (par2, cch * DS + s_i, scan_mid)

                            def e_acc(gp, s_i=s_i, par2=par2,
                                      sct=scan_tags[-1], pg=prev_gate_tag):
                                gp.wait_ge(s_dv.h, sct)
                                if s_i == 0 and pg is not None:
                                    gp.wait_ge(s_dv.h, pg)
                                if s_i == 0:
                                    gp.tensor_copy(
                                        out=ys32[:],
                                        in_=hctmp[:, par2 * 1024:
                                                  (par2 + 1) * 1024]
                                    ).then_inc(s_gp.h, 1)
                                else:
                                    gp.tensor_add(
                                        ys32[:], ys32[:],
                                        hctmp[:, par2 * 1024:(par2 + 1) * 1024]
                                    ).then_inc(s_gp.h, 1)
                            on("gpsimd")(e_acc)
                            pool_tags.append(s_gp.bump())

                        def e_gate(v, cch=cch, bi=bi, gt=pool_tags[-1],
                                   ly=last_yst):
                            v.wait_ge(s_gp.h, gt)
                            if cch == 0 and ly is not None:
                                v.wait_ge(s_yst.h, ly)
                            v.scalar_tensor_tensor(
                                out=ys32[:],
                                in0=xc_bf[:, (bi * NCH + cch) * 1024:
                                          (bi * NCH + cch + 1) * 1024],
                                scalar=D_sb[:, cch:cch + 1], in1=ys32[:],
                                op0=ALU.mult, op1=ALU.add)
                            v.tensor_tensor(
                                out=y_bf[:, cch * 1024:(cch + 1) * 1024],
                                in0=ys32[:],
                                in1=sz_bf[:, (bi * NCH + cch) * 1024:
                                          (bi * NCH + cch + 1) * 1024],
                                op=ALU.mult).then_inc(s_dv.h, 1)
                        on("vector")(e_gate)
                        prev_gate_tag = s_dv.bump()
                        last_gate[bi] = prev_gate_tag

                    if pend_carry is not None:
                        emit_carry(pend_carry)
                        pend_carry = None

                    y_tag = s_dv.n

                    def e_yst(s, b=b, y_tag=y_tag):
                        s.wait_ge(s_dv.h, y_tag)
                        cj = b // 8
                        j = b % 8
                        s.dma_start(
                            out=a2a_in[cj][j * DSH:(j + 1) * DSH, :].rearrange(
                                "(q p) t -> p q t", p=128),
                            in_=y_bf[:].rearrange("p (q t) -> p q t", t=1024)
                            ).then_inc(s_yst.h, 16)
                    on("sync")(e_yst)
                    last_yst = s_yst.bump(16)

                    if dev:
                        def e_ydev(s, b=b):
                            s.dma_start(
                                out=y_dev[:, b * TB:(b + 1) * TB].rearrange(
                                    "(q p) t -> p q t", p=128),
                                in_=y_bf[:].rearrange("p (q t) -> p q t", t=1024)
                                ).then_inc(s_yst.h, 16)
                        on("sync")(e_ydev)
                        last_yst = s_yst.bump(16)

                    if b % 8 == 7:
                        yst_tag = s_yst.n

                        def e_a2a(gp, b=b, yst_tag=yst_tag):
                            gp.wait_ge(s_yst.h, yst_tag)
                            cj = b // 8
                            with allow_ici_all_to_all():
                                gp.collective_compute(
                                    "AllToAll", ALU.bypass,
                                    replica_groups=[list(range(NCORES))],
                                    ins=[a2a_in[cj]], outs=[a2a_out[cj]]
                                ).then_inc(s_cc.h, 1)
                        on("gpsimd")(e_a2a)
                        a2a_tag.append(s_cc.bump())

        if stage in ("scan", "scanA", "scanB", "scanA0", "scanA1"):
            fins = [(s_cc, s_cc.n), (s_arst, s_arst.n), (s_yst, s_yst.n),
                    (s_bcc0, s_bcc0.n), (s_bcc1, s_bcc1.n), (s_dev, s_dev.n)]
            dv_fin = s_dv.n

            def finsc(g):
                for sem, tag in fins:
                    g.wait_ge(sem.h, tag)

            def finsc2(v):
                v.wait_ge(s_dv.h, dv_fin)
            on("gpsimd")(finsc)
            on("vector")(finsc2)
            scan_es.close()
            return nc

        scan_es.close()
        phase_barrier()

        ds_es = ExitStack()
        bigA = ds_es.enter_context(nc.sbuf_tensor("bigA", [128, NFI * 1024], BF16))
        big0 = bigA[:, :KI * 1024]
        big1 = bigA
        t2u1 = ds_es.enter_context(nc.sbuf_tensor("t2u1", [128, KD * 1024], BF16))
        wsl = ds_es.enter_context(nc.sbuf_tensor("wsl", [128, 2 * NFI * 128], BF16))
        mBd = ds_es.enter_context(nc.sbuf_tensor("mBd", [128, 1024], F32))
        rBd = ds_es.enter_context(nc.sbuf_tensor("rBd", [128, 1024], F32))
        xt1 = ds_es.enter_context(nc.sbuf_tensor("xt1", [128, 1024], F32))
        xt2 = ds_es.enter_context(nc.sbuf_tensor("xt2", [128, 1024], F32))
        cb3 = ds_es.enter_context(nc.sbuf_tensor("cb3", [128, 2 * 512], BF16))
        sq3 = ds_es.enter_context(nc.sbuf_tensor("sq3", [128, 2 * 512], BF16))
        sgt = ds_es.enter_context(nc.sbuf_tensor("sgt", [128, 1024], BF16))
        strow3 = ds_es.enter_context(nc.sbuf_tensor("strow3", [1, 1536], F32))
        ps_m0 = ds_es.enter_context(nc.psum_tensor("ps_m0", [128, 512], F32))
        ps_m1 = ds_es.enter_context(nc.psum_tensor("ps_m1", [128, 512], F32))
        ps_su2 = ds_es.enter_context(nc.psum_tensor("ps_su2", [1, 512], F32))
        ps_sq2 = ds_es.enter_context(nc.psum_tensor("ps_sq2", [1, 512], F32))
        if True:
            psm = [ps_m0, ps_m1]
            WSLH = NFI * 128

            def wseg_tile(seg, oc, nk):
                o = WSEG[seg] + oc * nk * 128
                return wbf[:, o:o + nk * 128]

            last_evac = {0: None, 1: None}
            last_wsl = {0: None, 1: None}

            s_wp = {0: s_w0, 1: s_w1}

            def mm_phase(pj, seg, noc, nk, rhs_sb, rhs_w, evac_fn,
                         extra_mm_wait=None):
                for oc in range(noc):
                    wpar = oc % 2

                    def d_wld(s, seg=seg, oc=oc, nk=nk, wpar=wpar,
                              first=(oc == 0), lw=last_wsl[wpar]):
                        if first and pj == 0 and seg == "op":
                            s.wait_ge(s_wc.h, wc_tag)
                        if lw is not None:
                            s.wait_ge(s_pe.h, lw)
                        s.dma_start(out=wsl[:, wpar * WSLH:wpar * WSLH + nk * 128],
                                    in_=wseg_tile(seg, oc, nk)
                                    ).then_inc(s_wp[wpar].h, 16)
                    on("sync")(d_wld)
                    w_tag = s_wp[wpar].bump(16)

                    for half in range(2):
                        pp = half
                        fw = last_evac[pp]

                        def d_mm(t, oc=oc, nk=nk, wpar=wpar, half=half,
                                 w_tag=w_tag, fw=fw, em=extra_mm_wait):
                            t.wait_ge(s_wp[wpar].h, w_tag)
                            if em is not None:
                                em(t)
                            if fw is not None:
                                kind, tg = fw
                                t.wait_ge(s_dv.h if kind == "dv" else s_ac.h, tg)
                            for k in range(nk):
                                i = t.matmul(psm[half][:],
                                             wsl[:, wpar * WSLH + k * 128:
                                                 wpar * WSLH + (k + 1) * 128],
                                             rhs_sb[:, k * rhs_w + half * 512:
                                                    k * rhs_w + half * 512 + 512],
                                             start=(k == 0), stop=(k == nk - 1))
                            i.then_inc(s_pe.h, 1)
                        on("tensor")(d_mm)
                        mm_tag = s_pe.bump()
                        last_wsl[wpar] = mm_tag
                        last_evac[pp] = evac_fn(oc, half, psm[half], mm_tag)

            mBd_use = [None]

            def bcast_pair(mrow, rrow, base, tgt_m, tgt_r, wait_st):
                def d_bc(sc, mrow=mrow, rrow=rrow, base=base, wait_st=wait_st,
                         war=mBd_use[0]):
                    sc.wait_ge(s_stst.h, wait_st)
                    if war is not None:
                        sc.wait_ge(s_dv.h, war)
                    sc.dma_start(out=tgt_m[:], in_=statbuf[mrow:mrow + 1,
                                                           base:base + TB]
                                 .to_broadcast([128, TB])).then_inc(s_bcd.h, 16)
                    sc.dma_start(out=tgt_r[:], in_=statbuf[rrow:rrow + 1,
                                                           base:base + TB]
                                 .to_broadcast([128, TB])).then_inc(s_bcd.h, 16)
                on("scalar")(d_bc)
                return s_bcd.bump(32)

            def stats_pass(src_dram, base, mrow, rrow, extra_mr=None,
                           wst=None):
                for half in range(2):
                    for k in range(KD):
                        prev_cast = s_ac.n

                        def d_l(s, k=k, half=half, base=base, src=src_dram,
                                prev_cast=prev_cast, wst=wst):
                            s.wait_ge(s_ac.h, prev_cast)
                            if wst is not None:
                                s.wait_ge(wst[0].h, wst[1])
                            s.dma_start(
                                out=xt1[:, 0:512],
                                in_=src[k * 128:(k + 1) * 128,
                                        base + half * 512:base + half * 512 + 512]
                            ).then_inc(s_stl.h, 16)
                        on("sync")(d_l)
                        l_tag = s_stl.bump(16)

                        def e_cast(sc, k=k, l_tag=l_tag, par=k % 2):
                            sc.wait_ge(s_stl.h, l_tag)
                            sc.activation(out=cb3[:, par * 512:(par + 1) * 512],
                                          in_=xt1[:, 0:512], func=AF.Copy)
                            sc.activation(out=sq3[:, par * 512:(par + 1) * 512],
                                          in_=xt1[:, 0:512],
                                          func=AF.Square).then_inc(s_ac.h, 1)
                        on("scalar")(e_cast)
                        c_tag = s_ac.bump()

                        def e_mm(t, k=k, c_tag=c_tag, par=k % 2):
                            t.wait_ge(s_ac.h, c_tag)
                            t.matmul(ps_su2[:1, :], ones_sb[:],
                                     cb3[:, par * 512:(par + 1) * 512],
                                     start=(k == 0), stop=(k == KD - 1))
                            t.matmul(ps_sq2[:1, :], ones_sb[:],
                                     sq3[:, par * 512:(par + 1) * 512],
                                     start=(k == 0),
                                     stop=(k == KD - 1)).then_inc(s_pe.h, 1)
                        on("tensor")(e_mm)
                        s_pe.bump()
                    emit_stats_finalize(strow3, ps_su2, ps_sq2, mrow, rrow,
                                        base + half * 512, extra_mr_row=extra_mr)
                return s_stst.n

            def assemble_pass(srcA, srcB, base, mB_t, rB_t, dst, bc_tag,
                              dst_sem, wstB, dev_dst=None):
                prev_store = None
                for k in range(KD):
                    def d_l(s, k=k, base=base, srcA=srcA, srcB=srcB,
                            prev_store=prev_store, wstB=wstB):
                        if prev_store is not None:
                            s.wait_ge(dst_sem.h, prev_store)
                        if wstB is not None:
                            s.wait_ge(wstB[0].h, wstB[1])
                        s.dma_start(out=xt2[:],
                                    in_=srcA[k * 128:(k + 1) * 128,
                                             base:base + TB]).then_inc(s_asl.h, 16)
                        s.dma_start(out=xt1[:],
                                    in_=srcB[k * 128:(k + 1) * 128,
                                             base:base + TB]).then_inc(s_asl.h, 16)
                    on("sync")(d_l)
                    l_tag = s_asl.bump(32)

                    def e_a(v, k=k, l_tag=l_tag, bc_tag=bc_tag):
                        v.wait_ge(s_asl.h, l_tag)
                        v.wait_ge(s_bcd.h, bc_tag)
                        v.tensor_sub(xt1[:], xt1[:], mB_t[:])
                        v.tensor_mul(xt1[:], xt1[:], rB_t[:])
                        v.tensor_add(xt2[:], xt2[:], xt1[:]).then_inc(s_dv.h, 1)
                    on("vector")(e_a)
                    a_tag = s_dv.bump()

                    def d_s(s, k=k, base=base, a_tag=a_tag, dst=dst,
                            dev_dst=dev_dst):
                        s.wait_ge(s_dv.h, a_tag)
                        s.dma_start(out=dst[k * 128:(k + 1) * 128, base:base + TB],
                                    in_=xt2[:]).then_inc(dst_sem.h, 16)
                        if dev_dst is not None:
                            s.dma_start(out=dev_dst[k * 128:(k + 1) * 128,
                                                    base:base + TB],
                                        in_=xt2[:]).then_inc(dst_sem.h, 16)
                    on("sync")(d_s)
                    dst_sem.bump(32 if dev_dst is not None else 16)
                    prev_store = dst_sem.n
                return dst_sem.n

            def proj_tiles(src_dram, base, wcol_sb, bcol_sb, nwcol_sb, mr_row,
                           r_row, use_gain, wst=None):
                bt = bcast_pair(mr_row, r_row, base, mBd, rBd, s_stst.n)
                prev_p = None
                prev_pc = None
                for k in range(KD):
                    def d_l(s, k=k, base=base, src=src_dram, prev_p=prev_p,
                            wst=wst):
                        if prev_p is not None:
                            s.wait_ge(s_dv.h, prev_p)
                        if wst is not None:
                            s.wait_ge(wst[0].h, wst[1])
                        s.dma_start(out=xt1[:],
                                    in_=src[k * 128:(k + 1) * 128,
                                            base:base + TB]).then_inc(s_ptl.h, 16)
                    on("sync")(d_l)
                    l_tag = s_ptl.bump(16)

                    def e_p(v, k=k, l_tag=l_tag, bt=bt, use_gain=use_gain,
                            prev_pc=prev_pc):
                        v.wait_ge(s_ptl.h, l_tag)
                        v.wait_ge(s_bcd.h, bt)
                        if prev_pc is not None:
                            v.wait_ge(s_ac.h, prev_pc)
                        if use_gain:
                            v.scalar_tensor_tensor(
                                out=xt2[:], in0=xt1[:],
                                scalar=wcol_sb[:, k:k + 1], in1=rBd[:],
                                op0=ALU.mult, op1=ALU.mult)
                            v.scalar_tensor_tensor(
                                out=xt2[:], in0=mBd[:],
                                scalar=nwcol_sb[:, k:k + 1], in1=xt2[:],
                                op0=ALU.mult, op1=ALU.add).then_inc(s_dv.h, 1)
                        else:
                            v.tensor_sub(xt2[:], xt1[:], mBd[:])
                            v.tensor_mul(xt2[:], xt2[:], rBd[:]).then_inc(s_dv.h, 1)
                    on("vector")(e_p)
                    p_tag = s_dv.bump()
                    prev_p = p_tag

                    def e_pc(sc, k=k, p_tag=p_tag, use_gain=use_gain):
                        sc.wait_ge(s_dv.h, p_tag)
                        if use_gain:
                            i = sc.activation(
                                out=t2u1[:, k * 1024:(k + 1) * 1024],
                                in_=xt2[:], func=AF.Identity,
                                bias=bcol_sb[:, k:k + 1])
                        else:
                            i = sc.activation(
                                out=t2u1[:, k * 1024:(k + 1) * 1024],
                                in_=xt2[:], func=AF.Copy)
                        i.then_inc(s_ac.h, 1)
                    on("scalar")(e_pc)
                    prev_pc = s_ac.bump()
                return s_ac.n

            big0_free = None
            for pj in range(OWN):
                base = pj * TB

                def d_yld(s, pj=pj, bf=big0_free):
                    s.wait_ge(s_cc.h, a2a_tag[pj])
                    if bf is not None:
                        s.wait_ge(s_pe.h, bf)
                    s.dma_start(out=big0.rearrange(
                                    "p (k t) -> p k t", t=1024),
                                in_=a2a_out[pj][:, :].rearrange(
                                    "(k p) t -> p k t", p=128)
                                ).then_inc(s_yb.h, 16)
                on("sync")(d_yld)
                yl_tag = s_yb.bump(16)

                bc0 = bcast_pair(0, 1, base, mBd, rBd, s_stst.n)

                ph1_xp = {0: None, 1: None}
                ph1_xst = {0: None, 1: None}

                def evac_ph1(oc, half, ps, mm_tag):
                    s_x0d = s_x0d0 if half == 0 else s_x0d1

                    def d_x0(s, oc=oc, half=half, base=base,
                             pxp=ph1_xp[half], s_x0d=s_x0d):
                        if pxp is not None:
                            s.wait_ge(s_dv.h, pxp)
                        s.dma_start(out=xt1[:, half * 512:half * 512 + 512],
                                    in_=x0_fm[oc * 128:(oc + 1) * 128,
                                              base + half * 512:
                                              base + half * 512 + 512]
                                    ).then_inc(s_x0d.h, 16)
                    on("sync")(d_x0)
                    x0_tag = s_x0d.bump(16)

                    def d_xp(v, oc=oc, half=half, mm_tag=mm_tag, x0_tag=x0_tag,
                             bc0=bc0, pxst=ph1_xst[half], s_x0d=s_x0d):
                        v.wait_ge(s_pe.h, mm_tag)
                        v.wait_ge(s_x0d.h, x0_tag)
                        v.wait_ge(s_bcd.h, bc0)
                        if pxst is not None:
                            v.wait_ge(s_xst.h, pxst)
                        sl = slice(half * 512, half * 512 + 512)
                        v.tensor_sub(xt2[:, sl], xt1[:, sl], mBd[:, sl])
                        v.tensor_mul(xt2[:, sl], xt2[:, sl], rBd[:, sl])
                        v.tensor_tensor(out=xt2[:, sl], in0=xt2[:, sl],
                                        in1=ps[:], op=ALU.add).then_inc(s_dv.h, 1)
                    on("vector")(d_xp)
                    dv_tag = s_dv.bump()
                    ph1_xp[half] = dv_tag

                    def d_xst(s, oc=oc, half=half, base=base, dv_tag=dv_tag):
                        s.wait_ge(s_dv.h, dv_tag)
                        s.dma_start(
                            out=xprime[oc * 128:(oc + 1) * 128,
                                       base + half * 512:base + half * 512 + 512],
                            in_=xt2[:, half * 512:half * 512 + 512]
                        ).then_inc(s_xst.h, 16)
                        if dev:
                            s.dma_start(
                                out=xp_dev[oc * 128:(oc + 1) * 128,
                                           base + half * 512:
                                           base + half * 512 + 512],
                                in_=xt2[:, half * 512:half * 512 + 512]
                            ).then_inc(s_xst.h, 16)
                    on("sync")(d_xst)
                    s_xst.bump(32 if dev else 16)
                    ph1_xst[half] = s_xst.n
                    return ("dv", dv_tag)

                def ph1_wait(t, yl_tag=yl_tag):
                    t.wait_ge(s_yb.h, yl_tag)

                mm_phase(pj, "op", NOC, KI, big0, 1024, evac_ph1,
                         extra_mm_wait=ph1_wait)

                mBd_use[0] = s_dv.n
                stats_pass(xprime, base, 2, 3, extra_mr=4,
                           wst=(s_xst, s_xst.n))

                proj_tiles(xprime, base, ln2w_sb, ln2b_sb, nw2_sb, 4, 3, True,
                           wst=(s_xst, s_xst.n))
                mBd_use[0] = s_dv.n

                ph2_hb = [None]

                def evac_ph2(i, half, ps, mm_tag):
                    real_i, ag = divmod(i, 2)
                    if ag == 0:
                        def d_cpa(v, half=half, mm_tag=mm_tag):
                            v.wait_ge(s_pe.h, mm_tag)
                            v.tensor_copy(out=xt2[:, half * 512:half * 512 + 512],
                                          in_=ps[:]).then_inc(s_dv.h, 1)
                        on("vector")(d_cpa)
                        return ("dv", s_dv.bump())
                    else:
                        def d_sg(sc, half=half, mm_tag=mm_tag, ph=ph2_hb[0]):
                            sc.wait_ge(s_pe.h, mm_tag)
                            if ph is not None:
                                sc.wait_ge(s_dv.h, ph)
                            sc.activation(out=sgt[:, half * 512:half * 512 + 512],
                                          in_=ps[:],
                                          func=AF.Silu).then_inc(s_ac.h, 1)
                        on("scalar")(d_sg)
                        ac_tag = s_ac.bump()

                        def d_hb(v, real_i=real_i, half=half, ac_tag=ac_tag):
                            v.wait_ge(s_ac.h, ac_tag)
                            v.tensor_tensor(
                                out=big0[:, real_i * 1024 + half * 512:
                                         real_i * 1024 + half * 512 + 512],
                                in0=xt2[:, half * 512:half * 512 + 512],
                                in1=sgt[:, half * 512:half * 512 + 512],
                                op=ALU.mult).then_inc(s_dv.h, 1)
                        on("vector")(d_hb)
                        ph2_hb[0] = s_dv.bump()
                        return ("dv", ph2_hb[0])

                mm_phase(pj, "ffp", NFA * 2, KD, t2u1, 1024, evac_ph2)

                ph3_blk = {0: None, 1: None}
                ph3_bst = {0: None, 1: None}

                def evac_ph3(oc, half, ps, mm_tag):
                    s_xpl = s_xpl0 if half == 0 else s_xpl1

                    def d_xl(s, oc=oc, half=half, base=base,
                             pb=ph3_blk[half], s_xpl=s_xpl,
                             wxp=(s_xst, s_xst.n)):
                        if pb is not None:
                            s.wait_ge(s_dv.h, pb)
                        s.wait_ge(wxp[0].h, wxp[1])
                        s.dma_start(out=xt1[:, half * 512:half * 512 + 512],
                                    in_=xprime[oc * 128:(oc + 1) * 128,
                                               base + half * 512:
                                               base + half * 512 + 512]
                                    ).then_inc(s_xpl.h, 16)
                    on("sync")(d_xl)
                    xl_tag = s_xpl.bump(16)

                    def d_blk(v, oc=oc, half=half, mm_tag=mm_tag, xl_tag=xl_tag,
                              pbs=ph3_bst[half], s_xpl=s_xpl):
                        v.wait_ge(s_pe.h, mm_tag)
                        v.wait_ge(s_xpl.h, xl_tag)
                        if pbs is not None:
                            v.wait_ge(s_bst.h, pbs)
                        sl = slice(half * 512, half * 512 + 512)
                        v.tensor_tensor(out=xt2[:, sl], in0=xt1[:, sl],
                                        in1=ps[:], op=ALU.add).then_inc(s_dv.h, 1)
                    on("vector")(d_blk)
                    dv_tag = s_dv.bump()
                    ph3_blk[half] = dv_tag

                    def d_bst(s, oc=oc, half=half, base=base, dv_tag=dv_tag):
                        s.wait_ge(s_dv.h, dv_tag)
                        s.dma_start(
                            out=blkbuf[oc * 128:(oc + 1) * 128,
                                       base + half * 512:base + half * 512 + 512],
                            in_=xt2[:, half * 512:half * 512 + 512]
                        ).then_inc(s_bst.h, 16)
                    on("sync")(d_bst)
                    s_bst.bump(16)
                    ph3_bst[half] = s_bst.n
                    return ("dv", dv_tag)

                mm_phase(pj, "ffo", NOC, KI, big0, 1024, evac_ph3)

                stats_pass(blkbuf, base, 5, 6, wst=(s_bst, s_bst.n))
                bcb = bcast_pair(5, 6, base, mBd, rBd, s_stst.n)
                assemble_pass(x0_fm, blkbuf, base, mBd, rBd, x1buf, bcb,
                              s_x1s, (s_bst, s_bst.n),
                              dev_dst=(x1_dev if dev else None))
                mBd_use[0] = s_dv.n

                stats_pass(x1buf, base, 7, 8, wst=(s_x1s, s_x1s.n))
                proj_tiles(x1buf, base, None, None, None, 7, 8, False,
                           wst=(s_x1s, s_x1s.n))
                mBd_use[0] = s_dv.n

                ph4_hd = {0: None, 1: None}

                def evac_ph4(i, half, ps, mm_tag):
                    real_i, gu = divmod(i, 2)
                    sl = slice(half * 512, half * 512 + 512)
                    if gu == 0:
                        def d_sg2(sc, real_i=real_i, mm_tag=mm_tag, sl=sl,
                                  phd=ph4_hd[half]):
                            sc.wait_ge(s_pe.h, mm_tag)
                            if phd is not None:
                                sc.wait_ge(s_dv.h, phd)
                            sc.activation(out=sgt[:, sl], in_=ps[:], func=AF.Silu,
                                          scale=gs_sb[:, real_i:real_i + 1]
                                          ).then_inc(s_ac.h, 1)
                        on("scalar")(d_sg2)
                        return ("ac", s_ac.bump())
                    else:
                        def d_hd(v, real_i=real_i, half=half, mm_tag=mm_tag,
                                 at=s_ac.n, sl=sl):
                            v.wait_ge(s_pe.h, mm_tag)
                            v.wait_ge(s_ac.h, at)
                            v.scalar_tensor_tensor(
                                out=big1[:, real_i * 1024 + half * 512:
                                         real_i * 1024 + half * 512 + 512],
                                in0=ps[:], scalar=us_sb[:, real_i:real_i + 1],
                                in1=sgt[:, sl], op0=ALU.mult,
                                op1=ALU.mult).then_inc(s_dv.h, 1)
                        on("vector")(d_hd)
                        ph4_hd[half] = s_dv.bump()
                        return ("dv", ph4_hd[half])

                for i2 in range(NFI):
                    for gu, seg in ((0, "g"), (1, "u")):
                        wpar = (i2 * 2 + gu) % 2

                        def d_wld2(s, seg=seg, i2=i2, wpar=wpar,
                                   lw=last_wsl[wpar]):
                            if lw is not None:
                                s.wait_ge(s_pe.h, lw)
                            s.dma_start(
                                out=wsl[:, wpar * WSLH:wpar * WSLH + KD * 128],
                                in_=wseg_tile(seg, i2, KD)
                                ).then_inc(s_wp[wpar].h, 16)
                        on("sync")(d_wld2)
                        w_tag = s_wp[wpar].bump(16)

                        for half in range(2):
                            pp = half
                            fw = last_evac[pp]

                            def d_mm2(t, wpar=wpar, half=half, w_tag=w_tag,
                                      fw=fw):
                                t.wait_ge(s_wp[wpar].h, w_tag)
                                if fw is not None:
                                    kind, tg = fw
                                    t.wait_ge(s_dv.h if kind == "dv" else s_ac.h,
                                              tg)
                                for k in range(KD):
                                    i = t.matmul(psm[half][:],
                                                 wsl[:, wpar * WSLH + k * 128:
                                                     wpar * WSLH + (k + 1) * 128],
                                                 t2u1[:, k * 1024 + half * 512:
                                                      k * 1024 + half * 512 + 512],
                                                 start=(k == 0),
                                                 stop=(k == KD - 1))
                                i.then_inc(s_pe.h, 1)
                            on("tensor")(d_mm2)
                            mm_tag = s_pe.bump()
                            last_wsl[wpar] = mm_tag
                            last_evac[pp] = evac_ph4(i2 * 2 + gu, half,
                                                     psm[half], mm_tag)

                ph5_fst = {0: None, 1: None}

                def evac_ph5(oc, half, ps, mm_tag):
                    def d_f(v, oc=oc, half=half, mm_tag=mm_tag,
                            pf=ph5_fst[half]):
                        v.wait_ge(s_pe.h, mm_tag)
                        if pf is not None:
                            v.wait_ge(s_fst.h, pf)
                        v.tensor_scalar(out=xt2[:, half * 512:half * 512 + 512],
                                        in0=ps[:],
                                        scalar1=dns_sb[:, oc:oc + 1],
                                        scalar2=None,
                                        op0=ALU.mult).then_inc(s_dv.h, 1)
                    on("vector")(d_f)
                    dv_tag = s_dv.bump()

                    def d_fst(s, oc=oc, half=half, base=base, dv_tag=dv_tag):
                        s.wait_ge(s_dv.h, dv_tag)
                        s.dma_start(
                            out=fbuf[oc * 128:(oc + 1) * 128,
                                     base + half * 512:base + half * 512 + 512],
                            in_=xt2[:, half * 512:half * 512 + 512]
                        ).then_inc(s_fst.h, 16)
                    on("sync")(d_fst)
                    s_fst.bump(16)
                    ph5_fst[half] = s_fst.n
                    return ("dv", dv_tag)

                mm_phase(pj, "dn", NOC, NFI, big1, 1024, evac_ph5)
                big0_free = s_pe.n
                mBd_use[0] = s_dv.n

                stats_pass(fbuf, base, 10, 11, wst=(s_fst, s_fst.n))
                bcf = bcast_pair(10, 11, base, mBd, rBd, s_stst.n)

                prev_os = None
                for k in range(KD):
                    def d_l2(s, k=k, base=base, prev_os=prev_os,
                             wf=(s_fst, s_fst.n), wx1=(s_x1s, s_x1s.n)):
                        if prev_os is not None:
                            s.wait_ge(s_out.h, prev_os)
                        s.wait_ge(wf[0].h, wf[1])
                        s.wait_ge(wx1[0].h, wx1[1])
                        s.dma_start(out=xt2[:],
                                    in_=x1buf[k * 128:(k + 1) * 128,
                                              base:base + TB]).then_inc(s_asl.h, 16)
                        s.dma_start(out=xt1[:],
                                    in_=fbuf[k * 128:(k + 1) * 128,
                                             base:base + TB]).then_inc(s_asl.h, 16)
                    on("sync")(d_l2)
                    l_tag = s_asl.bump(32)

                    def e_x2(v, k=k, l_tag=l_tag, bcf=bcf):
                        v.wait_ge(s_asl.h, l_tag)
                        v.wait_ge(s_bcd.h, bcf)
                        v.tensor_sub(xt1[:], xt1[:], mBd[:])
                        v.tensor_mul(xt1[:], xt1[:], rBd[:])
                        v.tensor_add(xt2[:], xt2[:], xt1[:]).then_inc(s_dv.h, 1)
                    on("vector")(e_x2)
                    a_tag = s_dv.bump()

                    def d_os(s, k=k, base=base, a_tag=a_tag):
                        s.wait_ge(s_dv.h, a_tag)
                        s.dma_start(out=out_fm[k * 128:(k + 1) * 128,
                                               base:base + TB],
                                    in_=xt2[:]).then_inc(s_out.h, 16)
                    on("sync")(d_os)
                    prev_os = s_out.bump(16)

                mBd_use[0] = s_dv.n

        ds_es.close()
        out_total = s_out.n

        def fin(g):
            g.wait_ge(s_out.h, out_total)
        on("gpsimd")(fin)

    return nc


def slab_layout(wT, nk, noc, oc_major=True):
    K, N = wT.shape
    assert K == nk * 128 and N == noc * 128, (wT.shape, nk, noc)
    w = wT.reshape(nk, 128, noc, 128)
    if oc_major:
        w = w.transpose(1, 2, 0, 3)
    else:
        w = w.transpose(1, 0, 2, 3)
    return np.ascontiguousarray(w.reshape(128, -1).astype(np.float32))


def chunk_major(vec, nch):
    v = np.asarray(vec, dtype=np.float32)
    if v.ndim == 1:
        return np.ascontiguousarray(v.reshape(nch, 128).T)
    M = v.shape[1]
    return np.ascontiguousarray(
        v.reshape(nch, 128, M).transpose(1, 0, 2).reshape(128, nch * M))


_NC_CACHE = {}


def _get_nc(dev=False):
    if dev not in _NC_CACHE:
        _NC_CACHE[dev] = build_nc(dev)
    return _NC_CACHE[dev]


def _prep_inputs(inputs):
    f32 = np.float32
    x0 = np.asarray(inputs["L0_attn_x"], f32)
    x0T = np.ascontiguousarray(x0.reshape(GT, DM).T)

    ipT = np.ascontiguousarray(np.asarray(inputs["L0_attn_in_proj_w"], f32).T)
    xpTf = np.ascontiguousarray(np.asarray(inputs["L0_attn_x_proj_w"], f32).T)
    dtpTf = np.ascontiguousarray(np.asarray(inputs["L0_attn_dt_proj_w"], f32).T)
    opTf = slab_layout(np.ascontiguousarray(
        np.asarray(inputs["L0_attn_out_proj_w"], f32).T), KI, NOC, True)
    ffpT_full = np.ascontiguousarray(np.asarray(inputs["L0_attn_ff_proj_w"], f32).T)
    fa = slab_layout(ffpT_full[:, :FFA], KD, NFA, True)
    fg = slab_layout(ffpT_full[:, FFA:], KD, NFA, True)
    ffp = np.empty((128, NFA * 2 * KD * 128), f32)
    step = KD * 128
    for i in range(NFA):
        ffp[:, (2 * i) * step:(2 * i + 1) * step] = fa[:, i * step:(i + 1) * step]
        ffp[:, (2 * i + 1) * step:(2 * i + 2) * step] = fg[:, i * step:(i + 1) * step]
    ffoTf = slab_layout(np.ascontiguousarray(
        np.asarray(inputs["L0_attn_ff_out_w"], f32).T), KI, NOC, True)
    gTf = slab_layout(np.ascontiguousarray(
        np.asarray(inputs["L0_ffn_gate_w"], f32).T), KD, NFI, True)
    uTf = slab_layout(np.ascontiguousarray(
        np.asarray(inputs["L0_ffn_up_w"], f32).T), KD, NFI, True)
    dnTf = slab_layout(np.ascontiguousarray(
        np.asarray(inputs["L0_ffn_down_w"], f32).T), NFI, NOC, True)

    convw_full = np.asarray(inputs["L0_attn_conv_w"], f32)[:, 0, :]
    ln1w = chunk_major(inputs["L0_attn_ln1_w"], KD)
    ln1b = chunk_major(inputs["L0_attn_ln1_b"], KD)
    ln2w = chunk_major(inputs["L0_attn_ln2_w"], KD)
    ln2b = chunk_major(inputs["L0_attn_ln2_b"], KD)
    gsv = chunk_major(inputs["L0_ffn_gate_s"], NFI)
    usv = chunk_major(inputs["L0_ffn_up_s"], NFI)
    dnsv = chunk_major(inputs["L0_ffn_down_s"], NOC)

    in_maps = []
    for k in range(NCORES):
        sh = slice(DSH * k, DSH * (k + 1))
        x0_fm = np.ascontiguousarray(np.concatenate(
            [x0T[:, (k + 8 * j) * TB:(k + 8 * j + 1) * TB] for j in range(OWN)],
            axis=1))
        wx = slab_layout(ipT[:, sh], KD, NCH, oc_major=False)
        wz = slab_layout(ipT[:, DI + DSH * k:DI + DSH * (k + 1)], KD, NCH,
                         oc_major=False)
        m = dict(
            x0_fm=x0_fm,
            wxT=wx, wzT=wz,
            xpT=chunk_major(xpTf[sh, :], NCH),
            dtpT=np.ascontiguousarray(dtpTf[:, sh]),
            opT=opTf, ffpT=ffp, ffoT=ffoTf, gT=gTf, uT=uTf, dnT=dnTf,
            convw=chunk_major(convw_full[sh, :], NCH),
            convb=chunk_major(np.asarray(inputs["L0_attn_conv_b"], f32)[sh], NCH),
            dtbv=chunk_major(np.asarray(inputs["L0_attn_dt_proj_b"], f32)[sh], NCH),
            alog=chunk_major(np.asarray(inputs["L0_attn_A_log"], f32)[sh, :], NCH),
            dskip=chunk_major(np.asarray(inputs["L0_attn_D_skip"], f32)[sh], NCH),
            ln1w=ln1w, ln1b=ln1b, ln2w=ln2w, ln2b=ln2b,
            gsv=gsv, usv=usv, dnsv=dnsv,
        )
        in_maps.append(m)
    return in_maps


def kernel(**inputs):
    nc = _get_nc(DEV)
    in_maps = _prep_inputs(inputs)
    res = run_bass_kernel_spmd(nc, in_maps, core_ids=list(range(NCORES)))
    out_full = np.empty((DM, GT), np.float32)
    for k in range(NCORES):
        o = res.results[k]["out_fm"]
        for j in range(OWN):
            out_full[:, (k + 8 * j) * TB:(k + 8 * j + 1) * TB] = \
                o[:, j * TB:(j + 1) * TB]
    out = np.ascontiguousarray(out_full.T).reshape(2, GT // 2, DM)
    if DEV:
        kernel.dev_results = res.results
    return out
